# version 18
# baseline (speedup 1.0000x reference)
import numpy as np
import concourse.bass as bass
import concourse.mybir as mybir
from concourse.bass_utils import run_bass_kernel_spmd
from contextlib import ExitStack

F32 = mybir.dt.float32
BF16 = mybir.dt.bfloat16
AF = mybir.ActivationFunctionType
ALU = mybir.AluOpType
AX = mybir.AxisListType

T = 4096
D = 1024
DI = 2048
NS = 16
E = 32
NJ = 34
NPAIR = NJ // 2
NQ = 4
QT = T // NQ
NCH = 8
EPS = 1e-5


class Ctr:
    def __init__(self, es, nc, name, owner=None):
        self.sem = es.enter_context(nc.semaphore(name))
        self.n = 0
        self.owner = owner

    def inc(self, inst, k=1):
        inst.then_inc(self.sem, k)
        self.n += k
        return (self, self.n)

    def tok(self):
        return (self, self.n)


class B:
    def __init__(self, nc, es):
        self.nc = nc
        self.es = es
        self.eng = {"T": nc.tensor, "V": nc.vector, "S": nc.scalar,
                    "P": nc.gpsimd, "Y": nc.sync}
        self.c = {k: Ctr(es, nc, "c" + k, owner=k) for k in "TVSP"}
        self.waited = {k: {} for k in "TVSPY"}

    def ctr(self, name):
        return Ctr(self.es, self.nc, name)

    def wait(self, ename, *toks):
        for tok in toks:
            if tok is None:
                continue
            if isinstance(tok, list):
                self.wait(ename, *tok)
                continue
            ctr, val = tok
            if val <= 0 or ctr.owner == ename:
                continue
            w = self.waited[ename]
            if w.get(id(ctr), (None, 0))[1] >= val:
                continue
            self.eng[ename].wait_ge(ctr.sem, val)
            w[id(ctr)] = (ctr, val)

    def join(self, extra=()):
        toks = [self.c[k].tok() for k in "TVSP"] + [c.tok() for c in extra]
        for ename in "TVSPY":
            self.wait(ename, *toks)


def build(nc, stages=("ffn1", "mamba", "ffn2"), taps=()):
    es = ExitStack()
    b = B(nc, es)
    T_, V_, S_, P_, Y_ = (b.eng[k] for k in "TVSPY")
    cT, cV, cS, cP = (b.c[k] for k in "TVSP")

    def dram_in(name, shape, dtype=F32):
        return nc.dram_tensor(name, list(shape), dtype, kind="ExternalInput").ap()

    def dram_int(name, shape, dtype=F32):
        return nc.dram_tensor(name, list(shape), dtype, kind="Internal").ap()

    d_x0 = dram_in("x0T", (D, T))
    d_out = nc.dram_tensor("outT", [D, T], F32, kind="ExternalOutput").ap()
    d_resid = dram_int("residT", (D, T))
    d_inW = dram_in("inWT", (D, 2 * DI))
    d_outW = dram_in("outWT", (DI, D))
    d_xpW = dram_in("xpWT", (DI, 33))
    d_gateW = dram_in("gateWt", (128, 256))
    d_shg = dram_in("shgT", (D, D))
    d_shu = dram_in("shuT", (D, D))
    d_shd = dram_in("shdT", (D, D))
    d_wg = dram_in("wgT", (E, D, 512))
    d_wu = dram_in("wuT", (E, D, 512))
    d_wd = dram_in("wdT", (E, 512, D))
    d_A = dram_in("At", (128, 256))
    d_convw = dram_in("convwT", (128, 64))
    d_convb = dram_in("convbT", (128, 16))
    d_D = dram_in("Dt", (128, 16))
    d_inB = dram_in("inBt", (128, 32))
    d_outB = dram_in("outBt", (128, 8))
    d_xpB = dram_in("xpBt", (33, 1))
    d_gamma = dram_in("gammat", (128, 8))
    d_beta = dram_in("betat", (128, 8))
    d_gateB = dram_in("gateBt", (1, 32))
    d_ones1 = dram_in("ones1", (1, 128))
    d_onecol = dram_in("onecol", (128, 1))
    d_ident = dram_in("ident", (128, 128))
    d_sel = dram_in("sel33", (33, 4096))
    d_zeros = dram_in("zeros", (128, 256))
    d_wcache = dram_int("wcache", (NJ, 128, 12288), BF16)
    d_mcin = dram_int("mcin", (32, 128, 1024), BF16)
    d_mcout = dram_int("mcout", (8, 128, 2048), BF16)

    sb = lambda n, sh, dt=F32: es.enter_context(nc.sbuf_tensor(n, sh, dt))
    ps = lambda n, sh: es.enter_context(nc.psum_tensor(n, sh, F32))

    c_ones1 = sb("c_ones1", [1, 128])
    c_onecol = sb("c_onecol", [128, 1])
    c_onecol_bf = sb("c_onecol_bf", [128, 1], BF16)
    c_ident = sb("c_ident", [128, 128])
    c_sel = sb("c_sel", [33, 4096], BF16)
    p_A = sb("p_A", [128, 256])
    p_convw = sb("p_convw", [128, 64])
    p_convb = sb("p_convb", [128, 16])
    p_D = sb("p_D", [128, 16])
    p_inB = sb("p_inB", [128, 32])
    p_outB = sb("p_outB", [128, 8])
    p_xpB = sb("p_xpB", [33, 1])
    p_gamma = sb("p_gamma", [128, 8])
    p_beta = sb("p_beta", [128, 8])
    p_gateB = sb("p_gateB", [1, 32])
    p_gbbc = sb("p_gbbc", [128, 32])
    p_gateW = sb("p_gateW", [128, 256])
    p_xpW = sb("p_xpW", [128, 528], BF16)

    pGU = [ps("pGU0", [128, 512]), ps("pGU1", [128, 512])]
    pDown = ps("pDown", [128, 2048])
    pMisc = ps("pMisc", [128, 512])
    pRow = ps("pRow", [33, 512])

    tap_state = {"n": 0}

    def tap(name, ap, shape, dtype=F32):
        if name not in taps:
            return
        dt_out = nc.dram_tensor("tap_" + name, list(shape), dtype,
                                kind="ExternalOutput").ap()
        b.wait("Y", cT.tok(), cV.tok(), cS.tok(), cP.tok())
        dSt.inc(Y_.dma_start(out=dt_out, in_=ap), 16)

    dInit = b.ctr("dInit")
    dInitP = b.ctr("dInitP")
    dX = b.ctr("dX")
    dSt = b.ctr("dSt")
    dW = [b.ctr(f"dW{i}") for i in range(3)]
    dWd = [b.ctr(f"dWd{i}") for i in range(4)]
    dFill = b.ctr("dFill")
    dMw = [b.ctr(f"dMw{i}") for i in range(4)]
    dMC = b.ctr("dMC")

    for dst, src in [(c_ones1, d_ones1), (c_onecol, d_onecol),
                     (c_ident, d_ident), (p_A, d_A),
                     (p_convw, d_convw), (p_convb, d_convb), (p_D, d_D),
                     (p_inB, d_inB), (p_outB, d_outB), (p_xpB, d_xpB),
                     (p_gamma, d_gamma), (p_beta, d_beta),
                     (p_gateB, d_gateB), (p_gateW, d_gateW)]:
        dInit.inc(Y_.dma_start(out=dst[:], in_=src[:]), 16)
    dInitP.inc(P_.dma_start(out=c_sel[:], in_=d_sel[:]), 16)
    dInitP.inc(P_.dma_start(out=c_onecol_bf[:], in_=d_onecol[:]), 16)
    dInitP.inc(P_.dma_start(
        out=p_xpW[:].rearrange("p (k c) -> p k c", c=33),
        in_=d_xpW.rearrange("(k p) c -> p k c", p=128)), 16)

    b.wait("S", dInit.tok(), dInitP.tok())
    t_exp = cS.inc(S_.activation(p_A[:], p_A[:], AF.Exp))
    b.wait("V", t_exp)
    t_A = cV.inc(V_.tensor_scalar_mul(p_A[:], p_A[:], -1.0))
    b.wait("T", dInit.tok(), dInitP.tok())
    t_mm = cT.inc(T_.matmul(pMisc[:, 0:32], c_ones1[:], p_gateB[:],
                            start=True, stop=True))
    b.wait("S", t_mm)
    t_gb = cS.inc(S_.copy(p_gbbc[:], pMisc[:, 0:32]))

    state = {"misc_free": t_gb, "row_free": None, "sq_free": None,
             "A_tok": t_A, "gb_tok": t_gb,
             "gu_free": [None, None], "down_free": None}

    def layer_norm(xin_ap, n_chunks, u32, rows, bcasts, sq, bank=None,
                   bank_key="misc_free"):
        r_mu, r_var, r_rstd = rows
        mu_bc, rstd_bc = bcasts
        nf = n_chunks * 128
        b.wait("T", state["row_free"])
        for m in range(n_chunks):
            inst = T_.matmul(pRow[0:1, :], c_onecol[:],
                             xin_ap[:, m * 512:(m + 1) * 512],
                             start=(m == 0), stop=(m == n_chunks - 1))
        t_sum = cT.inc(inst)
        b.wait("S", t_sum)
        t_mu = cS.inc(S_.activation(r_mu[:], pRow[0:1, :], AF.Copy,
                                    scale=1.0 / nf))
        b.wait("S", state["sq_free"])
        t_sq = cS.inc(S_.activation(sq[:, 0:n_chunks * 512],
                                    xin_ap[:, 0:n_chunks * 512], AF.Square))
        b.wait("T", t_mu, t_sq)
        for m in range(n_chunks):
            inst = T_.matmul(pRow[0:1, :], c_onecol_bf[:],
                             sq[:, m * 512:(m + 1) * 512],
                             start=(m == 0), stop=(m == n_chunks - 1))
        t_sum2 = cT.inc(inst)
        state["sq_free"] = t_sum2
        b.wait("S", t_sum2)
        t_msq = cS.inc(S_.activation(r_var[:], pRow[0:1, :], AF.Copy,
                                     scale=1.0 / nf))
        state["row_free"] = t_msq
        b.wait("V", t_mu, t_msq)
        V_.tensor_mul(r_rstd[:], r_mu[:], r_mu[:])
        V_.tensor_sub(r_var[:], r_var[:], r_rstd[:])
        t_var = cV.inc(V_.tensor_scalar_add(r_var[:], r_var[:], EPS))
        b.wait("S", t_var)
        S_.activation(r_rstd[:], r_var[:], AF.Ln)
        t_rstd = cS.inc(S_.activation(r_rstd[:], r_rstd[:], AF.Exp, scale=-0.5))
        bk = pMisc[:] if bank is None else bank
        b.wait("T", state[bank_key], t_mu)
        t_b1 = cT.inc(T_.matmul(bk, c_ones1[:], r_mu[:], start=True, stop=True))
        b.wait("S", t_b1)
        t_mubc = cS.inc(S_.copy(mu_bc[:], bk))
        b.wait("T", t_mubc, t_rstd)
        t_b2 = cT.inc(T_.matmul(bk, c_ones1[:], r_rstd[:], start=True, stop=True))
        b.wait("S", t_b2)
        t_rbc = cS.inc(S_.copy(rstd_bc[:], bk))
        state[bank_key] = t_rbc
        b.wait("V", t_mubc, t_rbc)
        for m in range(n_chunks):
            sl = slice(m * 512, (m + 1) * 512)
            V_.tensor_sub(u32[:, sl], xin_ap[:, sl], mu_bc[:])
            inst = V_.tensor_mul(u32[:, sl], u32[:, sl], rstd_bc[:])
        return cV.inc(inst)

    def ffn_call(src, dst, fill):
        f = ExitStack()
        fsb = lambda n, sh, dt=F32: f.enter_context(nc.sbuf_tensor(n, sh, dt))
        tag = "a" if fill else "b"
        ln_x = fsb("ln_x" + tag, [128, 8 * QT], BF16)
        racc = fsb("racc" + tag, [128, 8 * QT])
        combT = fsb("combT" + tag, [32, QT], BF16)
        xin = fsb("xin" + tag, [128, 4096])
        u32 = xin
        sq = fsb("sq" + tag, [128, 4096], BF16)
        wgu = [fsb(f"wgu{i}" + tag, [128, 8192], BF16) for i in range(3)]
        wds = [fsb(f"wds{i}" + tag, [128, 4096], BF16) for i in range(4)]
        silu_g = fsb("silu_g" + tag, [128, 2048], BF16)
        u_s = fsb("u_s" + tag, [128, 2048], BF16)
        gu = [fsb(f"gu{i}" + tag, [128, 4096], BF16) for i in range(2)]
        cbc = fsb("cbc" + tag, [128, 1024], BF16)
        rt = fsb("rt" + tag, [128, 1024])
        r_mu = fsb("r_mu" + tag, [1, 512])
        r_var = fsb("r_var" + tag, [1, 512])
        r_rstd = fsb("r_rstd" + tag, [1, 512])
        mu_bc = fsb("mu_bc" + tag, [128, 512])
        rstd_bc = fsb("rstd_bc" + tag, [128, 512])
        rows = (r_mu, r_var, r_rstd)
        bcasts = (mu_bc, rstd_bc)

        trk = {
            "xin": None, "u32": None, "lnx": [None, None],
            "silu_g": None, "u_s": None, "gu0": None, "gu1": None, "cbc": None,
            "combT": None, "rt_sig": None, "rt_tr": None,
            "slot_free": [None, None, None],
            "slot_tok": [None, None, None],
            "wd_free": [None] * 4, "wd_tok": [None] * 4,
        }

        def load_expert(L):
            s = L % 3
            j = L % NJ
            sd = L % 4
            if fill and L < NJ:
                b.wait("P", trk["slot_free"][s], trk["wd_free"][sd], dFill.tok())
                if j >= 2:
                    e = j - 2
                    srcs = [d_wg[e], d_wu[e], d_wd[e]]
                else:
                    srcs = [d_shg[:, j * 512:(j + 1) * 512],
                            d_shu[:, j * 512:(j + 1) * 512],
                            d_shd[j * 512:(j + 1) * 512, :]]
                for wi, s_ap in enumerate(srcs):
                    if wi < 2:
                        inap = s_ap.rearrange("(k p) i -> p k i", p=128)
                        outap = wgu[s][:, wi * 4096:(wi + 1) * 4096]\
                            .rearrange("p (k i) -> p k i", i=512)
                        dW[s].inc(P_.dma_start(out=outap, in_=inap), 16)
                    else:
                        inap = s_ap.rearrange("(k p) h -> p k h", p=128)
                        outap = wds[sd][:].rearrange("p (k h) -> p k h", h=1024)
                        dWd[sd].inc(P_.dma_start(out=outap, in_=inap), 16)
                b.wait("Y", dW[s].tok(), dWd[sd].tok())
                dFill.inc(Y_.dma_start(out=d_wcache[j][:, 0:8192], in_=wgu[s][:]), 16)
                dFill.inc(Y_.dma_start(out=d_wcache[j][:, 8192:12288], in_=wds[sd][:]), 16)
            else:
                b.wait("Y", trk["slot_free"][s], trk["wd_free"][sd])
                if fill:
                    b.wait("Y", dFill.tok())
                dW[s].inc(Y_.dma_start(out=wgu[s][:], in_=d_wcache[j][:, 0:8192]), 16)
                dWd[sd].inc(Y_.dma_start(out=wds[sd][:], in_=d_wcache[j][:, 8192:12288]), 16)
            trk["slot_tok"][s] = dW[s].tok()
            trk["wd_tok"][sd] = dWd[sd].tok()

        loaded = [0]

        def ensure(L):
            while loaded[0] <= L:
                load_expert(loaded[0])
                loaded[0] += 1

        ensure(2)

        for q in range(NQ):
            t_gates = [None, None]
            for tq in range(2):
                col0 = q * QT + tq * 512
                b.wait("Y", trk["xin"], dSt.tok())
                tx = dX.inc(Y_.dma_start(
                    out=xin[:].rearrange("p (m t) -> p m t", t=512),
                    in_=src[:, col0:col0 + 512].rearrange("(m p) t -> p m t", p=128)), 16)
                b.wait("T", tx)
                b.wait("S", tx)
                b.wait("V", trk["u32"])
                t_norm = layer_norm(xin, 8, u32, rows, bcasts, sq,
                                    bank=pDown[:, 0:512], bank_key="down_free")
                trk["xin"] = t_norm
                b.wait("S", t_norm, trk["lnx"][tq])
                lnap = ln_x[:].rearrange("p (m t) -> p m t", t=QT)[:, :, tq * 512:tq * 512 + 512]
                t_cast = cS.inc(S_.copy(lnap, u32[:].rearrange("p (m t) -> p m t", t=512)))
                trk["lnx"][tq] = t_cast
                b.wait("T", t_norm, state["misc_free"])
                for tt in range(4):
                    for m in range(8):
                        inst = T_.matmul(
                            pMisc[:, tq * 128 + tt * 32:tq * 128 + tt * 32 + 32],
                            u32[:, m * 512 + tt * 128: m * 512 + tt * 128 + 128],
                            p_gateW[:, m * 32:(m + 1) * 32],
                            start=(m == 0), stop=(m == 7))
                t_gates[tq] = cT.inc(inst)
                trk["xin"] = [t_cast, t_gates[tq], t_norm]
            state["misc_free"] = None

            def emit_routing(tq):
                t_gate = t_gates[tq]
                b.wait("S", t_gate, trk["rt_sig"], trk["rt_tr"])
                for tt in range(4):
                    S_.activation(rt[:, tt * 256:tt * 256 + 32],
                                  pMisc[:, tq * 128 + tt * 32:tq * 128 + tt * 32 + 32],
                                  AF.Sigmoid)
                t_sig = cS.inc(S_.drain())
                if tq == 1:
                    state["misc_free"] = t_sig

                def ro(tt, a, bb=None):
                    return rt[:, tt * 256 + a:tt * 256 + (bb if bb is not None else a + 1)]

                def ros(tt, a, bb, step):
                    return rt[:, tt * 256 + a:tt * 256 + bb:step]

                b.wait("V", t_sig, state["gb_tok"], trk["rt_tr"])
                for tt in range(4):
                    V_.tensor_add(ro(tt, 32, 64), ro(tt, 0, 32), p_gbbc[:])
                V_.drain()
                for tt in range(4):
                    for pi, (pa, pb) in enumerate(
                            [(0, 1), (0, 2), (0, 3), (1, 2), (1, 3), (2, 3)]):
                        V_.tensor_add(ro(tt, 64 + pi * 8, 72 + pi * 8),
                                      ros(tt, 32 + pa, 64, 4), ros(tt, 32 + pb, 64, 4))
                V_.drain()
                for tt in range(4):
                    V_.tensor_max(ro(tt, 112, 120), ro(tt, 64, 72), ro(tt, 72, 80))
                    V_.tensor_max(ro(tt, 120, 128), ro(tt, 80, 88), ro(tt, 88, 96))
                    V_.tensor_max(ro(tt, 240, 248), ro(tt, 96, 104), ro(tt, 104, 112))
                V_.drain()
                for tt in range(4):
                    V_.tensor_max(ro(tt, 112, 120), ro(tt, 112, 120), ro(tt, 120, 128))
                V_.drain()
                for tt in range(4):
                    V_.tensor_max(ro(tt, 112, 120), ro(tt, 112, 120), ro(tt, 240, 248))
                V_.drain()
                for tt in range(4):
                    V_.max(ro(tt, 120, 128), ro(tt, 112, 120))
                V_.drain()
                for tt in range(4):
                    V_.tensor_scalar(ro(tt, 128, 136), ro(tt, 112, 120),
                                     ro(tt, 123, 124), None, ALU.is_ge)
                V_.drain()
                for tt in range(4):
                    for a in range(4):
                        V_.tensor_copy(ros(tt, 136 + a, 168, 4), ro(tt, 128, 136))
                V_.drain()
                for tt in range(4):
                    V_.tensor_mul(ro(tt, 168, 200), ro(tt, 32, 64), ro(tt, 136, 168))
                V_.drain()
                for tt in range(4):
                    V_.max(ro(tt, 120, 128), ro(tt, 168, 200))
                V_.drain()
                for tt in range(4):
                    V_.tensor_scalar(ro(tt, 200, 232), ro(tt, 168, 200),
                                     ro(tt, 127, 128), None, ALU.is_ge)
                V_.drain()
                for tt in range(4):
                    V_.tensor_mul(ro(tt, 200, 232), ro(tt, 200, 232), ro(tt, 0, 32))
                V_.drain()
                for tt in range(4):
                    V_.reduce_sum(ro(tt, 254), ro(tt, 200, 232), AX.X)
                V_.drain()
                for tt in range(4):
                    V_.tensor_scalar_add(ro(tt, 254), ro(tt, 254), 1e-20)
                V_.drain()
                for tt in range(4):
                    V_.reciprocal(ro(tt, 255), ro(tt, 254))
                V_.drain()
                for tt in range(4):
                    V_.tensor_scalar(ro(tt, 160, 192), ro(tt, 200, 232),
                                     ro(tt, 255), 2.5, ALU.mult, ALU.mult)
                t_comb = cV.inc(V_.drain())
                trk["rt_sig"] = t_comb
                for tt in range(4):
                    b.wait("T", t_comb, state["row_free"])
                    t_tr = cT.inc(T_.transpose(pRow[0:32, 0:128],
                                               rt[:, tt * 256 + 160:tt * 256 + 192],
                                               c_ident[:]))
                    trk["rt_tr"] = t_tr
                    b.wait("S", t_tr, trk["combT"])
                    t_ct = cS.inc(S_.copy(
                        combT[:, tq * 512 + tt * 128:tq * 512 + tt * 128 + 128],
                        pRow[0:32, 0:128]))
                    state["row_free"] = t_ct


            if q == 0:
                tap("lnx", ln_x[:], (128, 8 * QT), BF16)
                tap("combT", combT[:], (32, QT), BF16)
            gu_free = state["gu_free"]
            pend = None

            def emit_down(p):
                P_i, tq_i, sd0_i, sd1_i, toks = p
                for t2 in range(2):
                    b.wait("T", state["down_free"], trk["wd_tok"][sd0_i],
                           trk["wd_tok"][sd1_i], *toks)
                    for hm in range(8):
                        for ji, sd in [(0, sd0_i), (1, sd1_i)]:
                            for kc in range(4):
                                inst = T_.matmul(
                                    pDown[:, hm * 256:(hm + 1) * 256],
                                    wds[sd][:, kc * 1024 + hm * 128:kc * 1024 + hm * 128 + 128],
                                    gu[tq_i][:, ji * 2048 + kc * 512 + t2 * 256:ji * 2048 + kc * 512 + t2 * 256 + 256],
                                    start=(ji == 0 and kc == 0),
                                    stop=(ji == 1 and kc == 3))
                    t_dn = cT.inc(inst)
                    trk["gu" + str(tq_i)] = t_dn
                    off = tq_i * 512 + t2 * 256
                    rap = racc[:].rearrange("p (h t) -> p h t", t=QT)[:, :, off:off + 256]
                    pap = pDown[:].rearrange("p (h t) -> p h t", t=256)
                    b.wait("V", t_dn)
                    if P_i == 0:
                        t_ra = cV.inc(V_.tensor_copy(rap, pap))
                    else:
                        t_ra = cV.inc(V_.tensor_tensor(rap, rap, pap, ALU.add))
                    state["down_free"] = t_ra
                trk["wd_free"][sd0_i] = cT.tok()
                trk["wd_free"][sd1_i] = cT.tok()

            for P in range(NPAIR):
                L0 = q * NJ + 2 * P
                ensure(L0 + 1)
                s0, s1 = L0 % 3, (L0 + 1) % 3
                sd0, sd1 = L0 % 4, (L0 + 1) % 4
                for tq in range(2):
                    gu_toks = []
                    for ji, (LL, s) in enumerate([(L0, s0), (L0 + 1, s1)]):
                        j = LL % NJ
                        real = j >= 2
                        sg_toks = []
                        for mi in range(4):
                            bank = pGU[mi % 2]
                            b.wait("T", gu_free[mi % 2], trk["slot_tok"][s],
                                   trk["lnx"][tq])
                            for k in range(8):
                                inst = T_.matmul(
                                    bank[:],
                                    wgu[s][:, k * 512 + mi * 128:k * 512 + mi * 128 + 128],
                                    ln_x[:, k * QT + tq * 512:k * QT + tq * 512 + 512],
                                    start=(k == 0), stop=(k == 7))
                            t_g = cT.inc(inst)
                            b.wait("S", t_g, trk["silu_g"])
                            t_sg = cS.inc(S_.activation(
                                silu_g[:, mi * 512:(mi + 1) * 512], bank[:], AF.Silu))
                            gu_free[mi % 2] = t_sg
                            sg_toks.append(t_sg)
                        if real:
                            e = j - 2
                            b.wait("T", state["misc_free"], trk["combT"])
                            t_mmc = cT.inc(T_.matmul(
                                pMisc[:], c_sel[0:32, e * 128:(e + 1) * 128],
                                combT[:, tq * 512:tq * 512 + 512],
                                start=True, stop=True))
                            b.wait("S", t_mmc, trk["cbc"])
                            t_cbc = cS.inc(S_.copy(cbc[:, ji * 512:(ji + 1) * 512],
                                                   pMisc[:]))
                            state["misc_free"] = t_cbc
                        else:
                            t_cbc = None
                        us_toks = []
                        for mi in range(4):
                            bank = pGU[mi % 2]
                            b.wait("T", gu_free[mi % 2])
                            for k in range(8):
                                inst = T_.matmul(
                                    bank[:],
                                    wgu[s][:, 4096 + k * 512 + mi * 128:4096 + k * 512 + mi * 128 + 128],
                                    ln_x[:, k * QT + tq * 512:k * QT + tq * 512 + 512],
                                    start=(k == 0), stop=(k == 7))
                            t_u = cT.inc(inst)
                            b.wait("V", t_u, trk["u_s"], t_cbc)
                            if real:
                                t_us = cV.inc(V_.tensor_mul(
                                    u_s[:, mi * 512:(mi + 1) * 512], bank[:],
                                    cbc[:, ji * 512:(ji + 1) * 512]))
                            else:
                                t_us = cV.inc(V_.tensor_copy(
                                    u_s[:, mi * 512:(mi + 1) * 512], bank[:]))
                            gu_free[mi % 2] = t_us
                            us_toks.append(t_us)
                        trk["cbc"] = us_toks[-1]
                        b.wait("V", trk["gu" + str(tq)], *sg_toks)
                        t_gu = cV.inc(V_.tensor_mul(
                            gu[tq][:, ji * 2048:(ji + 1) * 2048], silu_g[:], u_s[:]))
                        gu_toks.append(t_gu)
                        trk["silu_g"] = t_gu
                        trk["u_s"] = t_gu
                    if pend is not None:
                        emit_down(pend)
                    pend = (P, tq, sd0, sd1, gu_toks)
                if P == 0:
                    emit_routing(0)
                    emit_routing(1)
                trk["slot_free"][s0] = cT.tok()
                trk["slot_free"][s1] = cT.tok()
                ensure(L0 + 3)
            if pend is not None:
                emit_down(pend)
                pend = None
            state["gu_free"] = gu_free
            trk["combT"] = cT.tok()
            trk["lnx"] = [cT.tok(), cT.tok()]

            for tq in range(2):
                col0 = q * QT + tq * 512
                b.wait("Y", trk["xin"])
                tx = dX.inc(Y_.dma_start(
                    out=xin[:].rearrange("p (m t) -> p m t", t=512),
                    in_=src[:, col0:col0 + 512].rearrange("(m p) t -> p m t", p=128)), 16)
                rap = racc[:].rearrange("p (h t) -> p h t", t=QT)[:, :, tq * 512:tq * 512 + 512]
                b.wait("V", tx, state["down_free"], dSt.tok())
                t_xo = cV.inc(V_.scalar_tensor_tensor(
                    xin[:].rearrange("p (m t) -> p m t", t=512), rap, 0.5,
                    xin[:].rearrange("p (m t) -> p m t", t=512), ALU.mult, ALU.add))
                trk["xin"] = t_xo
                b.wait("Y", t_xo)
                dSt.inc(Y_.dma_start(
                    out=dst[:, col0:col0 + 512].rearrange("(m p) t -> p m t", p=128),
                    in_=xin[:].rearrange("p (m t) -> p m t", t=512)), 16)
                state["down_free"] = t_xo
        f.close()

    def mamba(mdst):
        f = ExitStack()
        fsb = lambda n, sh, dt=F32: f.enter_context(nc.sbuf_tensor(n, sh, dt))
        mw_in = [fsb(f"mw_in{i}", [128, 1024], BF16) for i in range(2)]
        mw_out = [fsb(f"mw_out{i}", [128, 2048], BF16) for i in range(2)]
        m_ubf = fsb("m_ubf", [128, 4096], BF16)
        xin = fsb("m_xin", [128, 4096])
        u32 = xin
        sq = fsb("m_sq", [128, 4096], BF16)
        m_xm2 = [fsb(f"m_xm{i}", [128, 515]) for i in range(2)]
        m_cacc2 = [fsb(f"m_cacc{i}", [128, 512]) for i in range(2)]
        m_zs = fsb("m_zs", [128, 8192], BF16)
        m_xc = fsb("m_xc", [128, 8192], BF16)
        m_ssm = fsb("m_ssm", [33, 512])
        m_ssm_bf = fsb("m_ssm_bf", [33, 512], BF16)
        m_dt = fsb("m_dt", [1, 512])
        m_Bbc = [fsb(f"m_Bbc{i}", [128, 512], BF16) for i in range(2)]
        m_Cbc = [fsb(f"m_Cbc{i}", [128, 512], BF16) for i in range(2)]
        m_dtB = [fsb(f"m_dtB{i}", [128, 512], BF16) for i in range(2)]
        m_dtbc = fsb("m_dtbc", [128, 512])
        m_dA4 = [fsb(f"m_dA4{i}", [128, 2048], BF16) for i in range(4)]
        m_dBx4 = [fsb(f"m_dBx4{i}", [128, 2048], BF16) for i in range(2)]
        m_h4 = [fsb(f"m_h4{i}", [128, 2048], BF16) for i in range(3)]
        m_hC4 = [fsb(f"m_hC4{i}", [128, 2048], BF16) for i in range(4)]

        def rep4(t512):
            ap = t512[:]
            return type(ap)(tensor=ap.tensor, offset=ap.offset,
                            ap=[list(ap.ap[0]), [0, 4], [1, 512]])

        def g3(ap2d):
            return ap2d.rearrange("p (g t) -> p g t", t=512)
        m_yacc = fsb("m_yacc", [128, 8192], BF16)
        m_yfin = m_yacc
        m_r32 = fsb("m_r32", [128, 4096])
        r_mu = fsb("mr_mu", [1, 512])
        r_var = fsb("mr_var", [1, 512])
        r_rstd = fsb("mr_rstd", [1, 512])
        m_rowa = r_rstd
        m_rowb = r_var
        mu_bc = fsb("mmu_bc", [128, 512])
        rstd_bc = fsb("mrstd_bc", [128, 512])
        rows = (r_mu, r_var, r_rstd)
        bcasts = (mu_bc, rstd_bc)
        m_carry = fsb("m_carry", [128, 256])
        m_tail = fsb("m_tail", [128, 48])
        dZ = b.ctr("dZ" + str(len(str(taps))))
        dZ.inc(Y_.dma_start(out=m_carry[:], in_=d_zeros[:, 0:256]), 16)
        dZ.inc(Y_.dma_start(out=m_tail[:], in_=d_zeros[:, 0:48]), 16)

        fill_free = [None, None, None, None]
        for m in range(32):
            s = m % 2
            b.wait("P", fill_free[s])
            dMw[s].inc(P_.dma_start(
                out=mw_in[s][:].rearrange("p (k c) -> p k c", c=128),
                in_=d_inW[:, m * 128:(m + 1) * 128].rearrange("(k p) c -> p k c", p=128)), 16)
            b.wait("Y", dMw[s].tok())
            fill_free[s] = dMC.inc(Y_.dma_start(out=d_mcin[m], in_=mw_in[s][:]), 16)
        for dm in range(8):
            s = dm % 2
            b.wait("P", fill_free[2 + s])
            dMw[2 + s].inc(P_.dma_start(
                out=mw_out[s][:].rearrange("p (k c) -> p k c", c=128),
                in_=d_outW[:, dm * 128:(dm + 1) * 128].rearrange("(k p) c -> p k c", p=128)), 16)
            b.wait("Y", dMw[2 + s].tok())
            fill_free[2 + s] = dMC.inc(Y_.dma_start(out=d_mcout[dm], in_=mw_out[s][:]), 16)
        b.wait("Y", dMC.tok())

        trk = {"xin": None, "u32": None, "ubf": None, "zs": None,
               "xc": None, "xm": [None, None], "cacc": [None, None],
               "tail": dZ.tok(),
               "slot": [None, None], "oslot": [None, None],
               "yacc": [None] * 16, "yfin": None, "r32": None, "ssm": None,
               "dA_free": [None] * 4, "dBx_free": [None, None],
               "h_free": [None] * 3, "hC_free": [None] * 4,
               "Bbc": [None, None], "Cbc": [None, None],
               "dtB": [None, None], "dtbc": None}
        zs_tok = [None] * 16
        b.wait("V", dZ.tok())

        for c in range(NCH):
            col0 = c * 512
            b.wait("Y", trk["xin"], dSt.tok())
            tx = dX.inc(Y_.dma_start(
                out=xin[:].rearrange("p (m t) -> p m t", t=512),
                in_=d_resid[:, col0:col0 + 512].rearrange("(m p) t -> p m t", p=128)), 16)
            b.wait("T", tx)
            b.wait("S", tx)
            b.wait("V", trk["u32"])
            t_norm = layer_norm(xin, 8, u32, rows, bcasts, sq)
            b.wait("S", t_norm, trk["ubf"])
            t_ubf = cS.inc(S_.copy(m_ubf[:], u32[:]))
            trk["u32"] = [t_ubf]
            b.wait("Y", t_ubf)
            tx = dX.inc(Y_.dma_start(
                out=xin[:].rearrange("p (m t) -> p m t", t=512),
                in_=d_resid[:, col0:col0 + 512].rearrange("(m p) t -> p m t", p=128)), 16)

            morder = [mm for pr in zip(range(16), range(16, 32)) for mm in pr]
            for mi_, m in enumerate(morder):
                s = mi_ % 2
                bank = pGU[s]
                b.wait("Y", trk["slot"][s])
                dMw[s].inc(Y_.dma_start(out=mw_in[s][:], in_=d_mcin[m]), 16)
                b.wait("T", dMw[s].tok(), t_ubf, state["gu_free"][s])
                for k in range(8):
                    inst = T_.matmul(bank[:], mw_in[s][:, k * 128:(k + 1) * 128],
                                     m_ubf[:, k * 512:(k + 1) * 512],
                                     start=(k == 0), stop=(k == 7))
                t_mmi = cT.inc(inst)
                trk["slot"][s] = t_mmi
                if m < 16:
                    k_ = m
                    xb2 = k_ % 2
                    m_xm = m_xm2[xb2]
                    m_cacc = m_cacc2[xb2]
                    b.wait("S", t_mmi, trk["xm"][xb2], trk["tail"],
                           trk["cacc"][xb2])
                    t_ev = cS.inc(S_.activation(m_xm[:, 3:515], bank[:],
                                                AF.Identity,
                                                bias=p_inB[:, m:m + 1]))
                    state["gu_free"][s] = t_ev
                    b.wait("V", t_ev, trk["tail"])
                    V_.tensor_copy(m_xm[:, 0:3], m_tail[:, k_ * 3:k_ * 3 + 3])
                    V_.tensor_scalar_mul(m_cacc[:], m_xm[:, 0:512],
                                         p_convw[:, k_ * 4:k_ * 4 + 1])
                    for jj in range(1, 4):
                        V_.scalar_tensor_tensor(
                            m_cacc[:], m_xm[:, jj:jj + 512],
                            p_convw[:, k_ * 4 + jj:k_ * 4 + jj + 1],
                            m_cacc[:], ALU.mult, ALU.add)
                    t_cv = cV.inc(V_.tensor_copy(m_tail[:, k_ * 3:k_ * 3 + 3],
                                                 m_xm[:, 512:515]))
                    trk["tail"] = t_cv
                    trk["xm"][xb2] = t_cv
                    b.wait("S", t_cv, trk["xc"])
                    t_xc = cS.inc(S_.activation(m_xc[:, k_ * 512:(k_ + 1) * 512],
                                                m_cacc[:], AF.Silu,
                                                bias=p_convb[:, k_:k_ + 1]))
                    trk["cacc"][xb2] = t_xc
                else:
                    k_ = m - 16
                    b.wait("S", t_mmi, trk["zs"])
                    t_z = cS.inc(S_.activation(m_zs[:, k_ * 512:(k_ + 1) * 512],
                                               bank[:], AF.Silu,
                                               bias=p_inB[:, m:m + 1]))
                    zs_tok[k_] = t_z
                    state["gu_free"][s] = t_z
            t_xc_all = [trk["cacc"][0], trk["cacc"][1]]

            b.wait("T", t_xc_all, state["row_free"], dInit.tok(), dInitP.tok())
            for k in range(16):
                inst = T_.matmul(pRow[:], p_xpW[:, k * 33:(k + 1) * 33],
                                 m_xc[:, k * 512:(k + 1) * 512],
                                 start=(k == 0), stop=(k == 15))
            t_ssm = cT.inc(inst)
            b.wait("V", t_ssm, trk["ssm"])
            t_ssme = cV.inc(V_.tensor_scalar_add(m_ssm[:], pRow[:], p_xpB[:]))
            state["row_free"] = t_ssme
            b.wait("S", t_ssme, trk["ssm"])
            t_ssmbf = cS.inc(S_.copy(m_ssm_bf[:], m_ssm[:]))
            b.wait("S", t_ssme)
            S_.activation(m_rowa[:], m_ssm[32:33, :], AF.Abs)
            S_.activation(m_rowa[:], m_rowa[:], AF.Exp, scale=-1.0)
            t_ln = cS.inc(S_.activation(m_rowa[:], m_rowa[:], AF.Ln, bias=1.0))
            b.wait("V", t_ln)
            V_.tensor_scalar_max(m_rowb[:], m_ssm[32:33, :], 0.0)
            t_dt = cV.inc(V_.tensor_add(m_dt[:], m_rowa[:], m_rowb[:]))
            b.wait("T", t_dt, state["misc_free"], trk["dtbc"])
            t_bcm = cT.inc(T_.matmul(pMisc[:], c_ones1[:], m_dt[:],
                                     start=True, stop=True))
            b.wait("S", t_bcm)
            t_dtbc = cS.inc(S_.copy(m_dtbc[:], pMisc[:]))
            state["misc_free"] = t_dtbc

            for k in range(16):
                b.wait("V", t_xc_all, trk["yacc"][k], state["A_tok"] if c == 0 else None)
                trk["yacc"][k] = cV.inc(V_.tensor_scalar_mul(
                    m_yacc[:, k * 512:(k + 1) * 512],
                    m_xc[:, k * 512:(k + 1) * 512], p_D[:, k:k + 1]))

            def prep_n(n):
                nb = n % 2
                b.wait("T", t_ssmbf, state["down_free"], trk["Bbc"][nb],
                       trk["Cbc"][nb])
                T_.matmul(pDown[:, 0:512], c_sel[:, n * 128:(n + 1) * 128],
                          m_ssm_bf[:], start=True, stop=True)
                t_bc1 = cT.inc(T_.matmul(pDown[:, 512:1024],
                                         c_sel[:, (16 + n) * 128:(17 + n) * 128],
                                         m_ssm_bf[:], start=True, stop=True))
                b.wait("S", t_bc1)
                t_B = cS.inc(S_.copy(m_Bbc[nb][:], pDown[:, 0:512]))
                t_C = cS.inc(S_.copy(m_Cbc[nb][:], pDown[:, 512:1024]))
                state["down_free"] = t_C
                return t_B, t_C

            def dtb_n(n, t_B):
                nb = n % 2
                b.wait("V", t_B, t_dtbc, trk["dtB"][nb])
                t_dtB = cV.inc(V_.tensor_mul(m_dtB[nb][:], m_Bbc[nb][:],
                                             m_dtbc[:]))
                trk["Bbc"][nb] = t_dtB
                return t_dtB

            tB0, tC0 = prep_n(0)
            t_dtB_cur = dtb_n(0, tB0)
            t_C_cur = tC0
            ti = 0
            for n in range(NS):
                nb = n % 2
                if n + 1 < NS:
                    nxt = prep_n(n + 1)
                for g4 in range(4):
                    db = ti % 4
                    xb = ti % 2
                    hb = ti % 3
                    cb = ti % 4
                    b.wait("S", t_dtbc, trk["dA_free"][db])
                    for kk in range(4):
                        idx = (g4 * 4 + kk) * 16 + n
                        inst = S_.activation(
                            m_dA4[db][:, kk * 512:(kk + 1) * 512],
                            m_dtbc[:], AF.Exp, scale=p_A[:, idx:idx + 1])
                    t_dA = cS.inc(inst)
                    b.wait("V", t_dtB_cur, t_xc_all, trk["dBx_free"][xb])
                    t_dBx = cV.inc(V_.tensor_tensor(
                        g3(m_dBx4[xb][:]),
                        g3(m_xc[:, g4 * 2048:(g4 + 1) * 2048]),
                        rep4(m_dtB[nb]), ALU.mult))
                    for kk in range(4):
                        k = g4 * 4 + kk
                        idx = k * 16 + n
                        b.wait("V", t_dA, trk["h_free"][hb])
                        t_h = cV.inc(V_.tensor_tensor_scan(
                            m_h4[hb][:, kk * 512:(kk + 1) * 512],
                            m_dA4[db][:, kk * 512:(kk + 1) * 512],
                            m_dBx4[xb][:, kk * 512:(kk + 1) * 512],
                            m_carry[:, idx:idx + 1], ALU.mult, ALU.add))
                        trk["dA_free"][db] = t_h
                    trk["dBx_free"][xb] = cV.tok()
                    for kk in range(4):
                        idx = (g4 * 4 + kk) * 16 + n
                        V_.tensor_copy(m_carry[:, idx:idx + 1],
                                       m_h4[hb][:, kk * 512 + 511:kk * 512 + 512])
                    t_hdone = cV.tok()
                    b.wait("V", t_C_cur, trk["hC_free"][cb])
                    t_hC = cV.inc(V_.tensor_tensor(
                        g3(m_hC4[cb][:]), g3(m_h4[hb][:]),
                        rep4(m_Cbc[nb]), ALU.mult))
                    trk["h_free"][hb] = t_hC
                    trk["Cbc"][nb] = t_hC
                    b.wait("V", t_hC, trk["yacc"][g4 * 4])
                    t_y = cV.inc(V_.tensor_tensor(
                        m_yacc[:, g4 * 2048:(g4 + 1) * 2048],
                        m_yacc[:, g4 * 2048:(g4 + 1) * 2048], m_hC4[cb][:], ALU.add))
                    for kk in range(4):
                        trk["yacc"][g4 * 4 + kk] = t_y
                    trk["hC_free"][cb] = t_y
                    ti += 1
                if n + 1 < NS:
                    t_dtB_cur = dtb_n(n + 1, nxt[0])
                    t_C_cur = nxt[1]
            trk["xc"] = [cV.tok(), cP.tok(), t_ssm]
            trk["dtbc"] = [cV.tok(), cS.tok()]
            trk["ssm"] = [cT.tok(), t_ln]

            for k in range(16):
                b.wait("V", trk["yacc"][k], zs_tok[k], trk["yfin"])
                cV.inc(V_.tensor_mul(m_yfin[:, k * 512:(k + 1) * 512],
                                     m_yacc[:, k * 512:(k + 1) * 512],
                                     m_zs[:, k * 512:(k + 1) * 512]))
            t_yf = cV.tok()
            trk["zs"] = t_yf
            for kk in range(16):
                trk["yacc"][kk] = t_yf
            for dm in range(8):
                s = dm % 2
                bank = pGU[s]
                b.wait("Y", trk["oslot"][s])
                dMw[2 + s].inc(Y_.dma_start(out=mw_out[s][:], in_=d_mcout[dm]), 16)
                b.wait("T", dMw[2 + s].tok(), t_yf, state["gu_free"][s])
                for k in range(16):
                    inst = T_.matmul(bank[:], mw_out[s][:, k * 128:(k + 1) * 128],
                                     m_yfin[:, k * 512:(k + 1) * 512],
                                     start=(k == 0), stop=(k == 15))
                t_op = cT.inc(inst)
                trk["oslot"][s] = t_op
                b.wait("V", t_op, trk["r32"], t_ubf)
                t_r = cV.inc(V_.scalar_tensor_tensor(
                    m_r32[:, dm * 512:(dm + 1) * 512], bank[:],
                    p_outB[:, dm:dm + 1], m_ubf[:, dm * 512:(dm + 1) * 512],
                    ALU.add, ALU.add))
                state["gu_free"][s] = t_r
            t_rall = cV.tok()
            trk["ubf"] = t_rall
            trk["yfin"] = cT.tok()
            b.wait("T", t_rall)
            b.wait("S", t_rall)
            t_n2 = layer_norm(m_r32, 8, m_r32, rows, bcasts, sq)
            trk["r32"] = [t_n2, cT.tok()]
            b.wait("V", t_n2, tx)
            for dm in range(8):
                sl = slice(dm * 512, (dm + 1) * 512)
                V_.tensor_scalar(m_r32[:, sl], m_r32[:, sl], p_gamma[:, dm:dm + 1],
                                 p_beta[:, dm:dm + 1], ALU.mult, ALU.add)
                inst = V_.tensor_add(xin[:, sl], m_r32[:, sl], xin[:, sl])
            t_x2 = cV.inc(inst)
            trk["u32"] = [t_x2]
            trk["xin"] = t_x2
            for kk in range(16):
                trk["yacc"][kk] = [t_x2, cT.tok(), cS.tok()]
            b.wait("Y", t_x2)
            dSt.inc(Y_.dma_start(
                out=mdst[:, col0:col0 + 512].rearrange("(m p) t -> p m t", p=128),
                in_=xin[:].rearrange("p (m t) -> p m t", t=512)), 16)
        f.close()

    with es, nc.Block() as blk:
        @blk.vector
        def _(v):
            last = stages[-1]
            ffn_call(d_x0, d_out if last == "ffn1" else d_resid, fill=True)
            if "mamba" in stages:
                b.join(extra=[dX, dSt, dFill] + dW + dWd)
                mamba(d_out if last == "mamba" else d_resid)
            if "ffn2" in stages:
                b.join(extra=[dX, dSt, dMC] + dMw)
                ffn_call(d_resid, d_out, fill=False)
            Y_.wait_ge(dSt.sem, dSt.n)
    return nc


def _host_prep(inp):
    asc = np.ascontiguousarray
    f = lambda a: np.asarray(a, np.float32)
    g = {}
    g["inWT"] = asc(f(inp["L0_attn_in_proj_w"]).T)
    g["outWT"] = asc(f(inp["L0_attn_out_proj_w"]).T)
    g["xpWT"] = asc(f(inp["L0_attn_x_proj_w"]).T)
    g["gateWt"] = asc(f(inp["L0_ffn_gate_w"]).T.reshape(8, 128, 32)
                      .transpose(1, 0, 2).reshape(128, 256))
    g["shgT"] = asc(f(inp["L0_ffn_sh_gate"]).T)
    g["shuT"] = asc(f(inp["L0_ffn_sh_up"]).T)
    g["shdT"] = asc(f(inp["L0_ffn_sh_down"]).T)
    g["wgT"] = asc(f(inp["L0_ffn_w_gate"]).transpose(0, 2, 1))
    g["wuT"] = asc(f(inp["L0_ffn_w_up"]).transpose(0, 2, 1))
    g["wdT"] = asc(f(inp["L0_ffn_w_down"]).transpose(0, 2, 1))
    g["At"] = asc(f(inp["L0_attn_A_log"]).reshape(16, 128, 16)
                  .transpose(1, 0, 2).reshape(128, 256))
    g["convwT"] = asc(f(inp["L0_attn_conv_w"]).reshape(2048, 4)
                      .reshape(16, 128, 4).transpose(1, 0, 2).reshape(128, 64))
    g["convbT"] = asc(f(inp["L0_attn_conv_b"]).reshape(16, 128).T)
    g["Dt"] = asc(f(inp["L0_attn_D"]).reshape(16, 128).T)
    g["inBt"] = asc(f(inp["L0_attn_in_proj_b"]).reshape(32, 128).T)
    g["outBt"] = asc(f(inp["L0_attn_out_proj_b"]).reshape(8, 128).T)
    g["xpBt"] = asc(f(inp["L0_attn_x_proj_b"]).reshape(33, 1))
    g["gammat"] = asc(f(inp["L0_attn_ln_gamma"]).reshape(8, 128).T)
    g["betat"] = asc(f(inp["L0_attn_ln_beta"]).reshape(8, 128).T)
    g["gateBt"] = asc(f(inp["L0_ffn_gate_bias"]).reshape(1, 32))
    g["ones1"] = np.ones((1, 128), np.float32)
    g["onecol"] = np.ones((128, 1), np.float32)
    g["ident"] = np.eye(128, dtype=np.float32)
    sel = np.zeros((33, 4096), np.float32)
    for r in range(32):
        sel[r, r * 128:(r + 1) * 128] = 1.0
    g["sel33"] = sel
    g["zeros"] = np.zeros((128, 256), np.float32)
    return g


def kernel(**inputs):
    shared = _host_prep(inputs)
    x = np.asarray(inputs["L0_attn_x"], np.float32)
    in_maps = []
    for bb in range(8):
        m = dict(shared)
        m["x0T"] = np.ascontiguousarray(x[bb].T)
        in_maps.append(m)
    nc = build(bass.Bass("TRN2", target_bir_lowering=False, debug=False))
    res = run_bass_kernel_spmd(nc, in_maps, core_ids=list(range(8)))
    out = np.stack([np.asarray(r["outT"], np.float32).T for r in res.results])
    return np.ascontiguousarray(out)


# version 19
# speedup vs baseline: 1.0732x; 1.0732x over previous
import numpy as np
import concourse.bass as bass
import concourse.mybir as mybir
from concourse.bass_utils import run_bass_kernel_spmd
from contextlib import ExitStack

F32 = mybir.dt.float32
BF16 = mybir.dt.bfloat16
AF = mybir.ActivationFunctionType
ALU = mybir.AluOpType
AX = mybir.AxisListType

T = 4096
D = 1024
DI = 2048
NS = 16
E = 32
NJ = 34
NPAIR = NJ // 2
NQ = 4
QT = T // NQ
NCH = 8
EPS = 1e-5


class Ctr:
    def __init__(self, es, nc, name, owner=None):
        self.sem = es.enter_context(nc.semaphore(name))
        self.n = 0
        self.owner = owner

    def inc(self, inst, k=1):
        inst.then_inc(self.sem, k)
        self.n += k
        return (self, self.n)

    def tok(self):
        return (self, self.n)


class B:
    def __init__(self, nc, es):
        self.nc = nc
        self.es = es
        self.eng = {"T": nc.tensor, "V": nc.vector, "S": nc.scalar,
                    "P": nc.gpsimd, "Y": nc.sync}
        self.c = {k: Ctr(es, nc, "c" + k, owner=k) for k in "TVSP"}
        self.waited = {k: {} for k in "TVSPY"}

    def ctr(self, name):
        return Ctr(self.es, self.nc, name)

    def wait(self, ename, *toks):
        for tok in toks:
            if tok is None:
                continue
            if isinstance(tok, list):
                self.wait(ename, *tok)
                continue
            ctr, val = tok
            if val <= 0 or ctr.owner == ename:
                continue
            w = self.waited[ename]
            if w.get(id(ctr), (None, 0))[1] >= val:
                continue
            self.eng[ename].wait_ge(ctr.sem, val)
            w[id(ctr)] = (ctr, val)

    def join(self, extra=()):
        toks = [self.c[k].tok() for k in "TVSP"] + [c.tok() for c in extra]
        for ename in "TVSPY":
            self.wait(ename, *toks)


def build(nc, stages=("ffn1", "mamba", "ffn2"), taps=()):
    es = ExitStack()
    b = B(nc, es)
    T_, V_, S_, P_, Y_ = (b.eng[k] for k in "TVSPY")
    cT, cV, cS, cP = (b.c[k] for k in "TVSP")

    def dram_in(name, shape, dtype=F32):
        return nc.dram_tensor(name, list(shape), dtype, kind="ExternalInput").ap()

    def dram_int(name, shape, dtype=F32):
        return nc.dram_tensor(name, list(shape), dtype, kind="Internal").ap()

    d_x0 = dram_in("x0T", (D, T))
    d_out = nc.dram_tensor("outT", [D, T], F32, kind="ExternalOutput").ap()
    d_resid = dram_int("residT", (D, T))
    d_inW = dram_in("inWT", (D, 2 * DI))
    d_outW = dram_in("outWT", (DI, D))
    d_xpW = dram_in("xpWT", (DI, 33))
    d_gateW = dram_in("gateWt", (128, 256))
    d_shg = dram_in("shgT", (D, D))
    d_shu = dram_in("shuT", (D, D))
    d_shd = dram_in("shdT", (D, D))
    d_wg = dram_in("wgT", (E, D, 512))
    d_wu = dram_in("wuT", (E, D, 512))
    d_wd = dram_in("wdT", (E, 512, D))
    d_A = dram_in("At", (128, 256))
    d_convw = dram_in("convwT", (128, 64))
    d_convb = dram_in("convbT", (128, 16))
    d_D = dram_in("Dt", (128, 16))
    d_inB = dram_in("inBt", (128, 32))
    d_outB = dram_in("outBt", (128, 8))
    d_xpB = dram_in("xpBt", (33, 1))
    d_gamma = dram_in("gammat", (128, 8))
    d_beta = dram_in("betat", (128, 8))
    d_gateB = dram_in("gateBt", (1, 32))
    d_ones1 = dram_in("ones1", (1, 128))
    d_onecol = dram_in("onecol", (128, 1))
    d_ident = dram_in("ident", (128, 128))
    d_sel = dram_in("sel33", (33, 4096))
    d_zeros = dram_in("zeros", (128, 256))
    d_wcache = dram_int("wcache", (NJ, 128, 12288), BF16)
    d_mcin = dram_int("mcin", (32, 128, 1024), BF16)
    d_mcout = dram_int("mcout", (8, 128, 2048), BF16)

    sb = lambda n, sh, dt=F32: es.enter_context(nc.sbuf_tensor(n, sh, dt))
    ps = lambda n, sh: es.enter_context(nc.psum_tensor(n, sh, F32))

    c_ones1 = sb("c_ones1", [1, 128])
    c_onecol = sb("c_onecol", [128, 1])
    c_onecol_bf = sb("c_onecol_bf", [128, 1], BF16)
    c_ident = sb("c_ident", [128, 128])
    c_sel = sb("c_sel", [33, 4096], BF16)
    p_A = sb("p_A", [128, 256])
    p_convw = sb("p_convw", [128, 64])
    p_convb = sb("p_convb", [128, 16])
    p_D = sb("p_D", [128, 16])
    p_inB = sb("p_inB", [128, 32])
    p_outB = sb("p_outB", [128, 8])
    p_xpB = sb("p_xpB", [33, 1])
    p_gamma = sb("p_gamma", [128, 8])
    p_beta = sb("p_beta", [128, 8])
    p_gateB = sb("p_gateB", [1, 32])
    p_gbbc = sb("p_gbbc", [128, 32])
    p_gateW = sb("p_gateW", [128, 256])
    p_xpW = sb("p_xpW", [128, 528], BF16)

    pGU = [ps("pGU0", [128, 512]), ps("pGU1", [128, 512])]
    pDown = ps("pDown", [128, 2048])
    pMisc = ps("pMisc", [128, 512])
    pRow = ps("pRow", [33, 512])

    tap_state = {"n": 0}

    def tap(name, ap, shape, dtype=F32):
        if name not in taps:
            return
        dt_out = nc.dram_tensor("tap_" + name, list(shape), dtype,
                                kind="ExternalOutput").ap()
        b.wait("Y", cT.tok(), cV.tok(), cS.tok(), cP.tok())
        dSt.inc(Y_.dma_start(out=dt_out, in_=ap), 16)

    dInit = b.ctr("dInit")
    dInitP = b.ctr("dInitP")
    dX = b.ctr("dX")
    dSt = b.ctr("dSt")
    dW = [b.ctr(f"dW{i}") for i in range(3)]
    dWd = [b.ctr(f"dWd{i}") for i in range(4)]
    dFill = b.ctr("dFill")
    dMw = [b.ctr(f"dMw{i}") for i in range(4)]
    dMC = b.ctr("dMC")

    for dst, src in [(c_ones1, d_ones1), (c_onecol, d_onecol),
                     (c_ident, d_ident), (p_A, d_A),
                     (p_convw, d_convw), (p_convb, d_convb), (p_D, d_D),
                     (p_inB, d_inB), (p_outB, d_outB), (p_xpB, d_xpB),
                     (p_gamma, d_gamma), (p_beta, d_beta),
                     (p_gateB, d_gateB), (p_gateW, d_gateW)]:
        dInit.inc(Y_.dma_start(out=dst[:], in_=src[:]), 16)
    dInitP.inc(P_.dma_start(out=c_sel[:], in_=d_sel[:]), 16)
    dInitP.inc(P_.dma_start(out=c_onecol_bf[:], in_=d_onecol[:]), 16)
    dInitP.inc(P_.dma_start(
        out=p_xpW[:].rearrange("p (k c) -> p k c", c=33),
        in_=d_xpW.rearrange("(k p) c -> p k c", p=128)), 16)

    b.wait("S", dInit.tok(), dInitP.tok())
    t_exp = cS.inc(S_.activation(p_A[:], p_A[:], AF.Exp))
    b.wait("V", t_exp)
    t_A = cV.inc(V_.tensor_scalar_mul(p_A[:], p_A[:], -1.0))
    b.wait("T", dInit.tok(), dInitP.tok())
    t_mm = cT.inc(T_.matmul(pMisc[:, 0:32], c_ones1[:], p_gateB[:],
                            start=True, stop=True))
    b.wait("S", t_mm)
    t_gb = cS.inc(S_.copy(p_gbbc[:], pMisc[:, 0:32]))

    state = {"misc_free": t_gb, "row_free": None, "sq_free": None,
             "A_tok": t_A, "gb_tok": t_gb,
             "gu_free": [None, None], "down_free": None}

    def layer_norm(xin_ap, n_chunks, u32, rows, bcasts, sq, bank=None,
                   bank_key="misc_free"):
        r_mu, r_var, r_rstd = rows
        mu_bc, rstd_bc = bcasts
        nf = n_chunks * 128
        b.wait("T", state["row_free"])
        for m in range(n_chunks):
            inst = T_.matmul(pRow[0:1, :], c_onecol[:],
                             xin_ap[:, m * 512:(m + 1) * 512],
                             start=(m == 0), stop=(m == n_chunks - 1))
        t_sum = cT.inc(inst)
        b.wait("S", t_sum)
        t_mu = cS.inc(S_.activation(r_mu[:], pRow[0:1, :], AF.Copy,
                                    scale=1.0 / nf))
        b.wait("S", state["sq_free"])
        t_sq = cS.inc(S_.activation(sq[:, 0:n_chunks * 512],
                                    xin_ap[:, 0:n_chunks * 512], AF.Square))
        b.wait("T", t_mu, t_sq)
        for m in range(n_chunks):
            inst = T_.matmul(pRow[0:1, :], c_onecol_bf[:],
                             sq[:, m * 512:(m + 1) * 512],
                             start=(m == 0), stop=(m == n_chunks - 1))
        t_sum2 = cT.inc(inst)
        state["sq_free"] = t_sum2
        b.wait("S", t_sum2)
        t_msq = cS.inc(S_.activation(r_var[:], pRow[0:1, :], AF.Copy,
                                     scale=1.0 / nf))
        state["row_free"] = t_msq
        b.wait("V", t_mu, t_msq)
        V_.tensor_mul(r_rstd[:], r_mu[:], r_mu[:])
        V_.tensor_sub(r_var[:], r_var[:], r_rstd[:])
        t_var = cV.inc(V_.tensor_scalar_add(r_var[:], r_var[:], EPS))
        b.wait("S", t_var)
        S_.activation(r_rstd[:], r_var[:], AF.Ln)
        t_rstd = cS.inc(S_.activation(r_rstd[:], r_rstd[:], AF.Exp, scale=-0.5))
        bk = pMisc[:] if bank is None else bank
        b.wait("T", state[bank_key], t_mu)
        t_b1 = cT.inc(T_.matmul(bk, c_ones1[:], r_mu[:], start=True, stop=True))
        b.wait("S", t_b1)
        t_mubc = cS.inc(S_.copy(mu_bc[:], bk))
        b.wait("T", t_mubc, t_rstd)
        t_b2 = cT.inc(T_.matmul(bk, c_ones1[:], r_rstd[:], start=True, stop=True))
        b.wait("S", t_b2)
        t_rbc = cS.inc(S_.copy(rstd_bc[:], bk))
        state[bank_key] = t_rbc
        b.wait("V", t_mubc, t_rbc)
        for m in range(n_chunks):
            sl = slice(m * 512, (m + 1) * 512)
            V_.tensor_sub(u32[:, sl], xin_ap[:, sl], mu_bc[:])
            inst = V_.tensor_mul(u32[:, sl], u32[:, sl], rstd_bc[:])
        return cV.inc(inst)

    def ffn_call(src, dst, fill):
        f = ExitStack()
        fsb = lambda n, sh, dt=F32: f.enter_context(nc.sbuf_tensor(n, sh, dt))
        tag = "a" if fill else "b"
        ln_x = fsb("ln_x" + tag, [128, 8 * QT], BF16)
        racc = fsb("racc" + tag, [128, 8 * QT])
        combT = fsb("combT" + tag, [32, QT], BF16)
        xin = fsb("xin" + tag, [128, 4096])
        u32 = xin
        sq = fsb("sq" + tag, [128, 4096], BF16)
        wgu = [fsb(f"wgu{i}" + tag, [128, 8192], BF16) for i in range(3)]
        wds = [fsb(f"wds{i}" + tag, [128, 4096], BF16) for i in range(4)]
        silu_g = fsb("silu_g" + tag, [128, 2048], BF16)
        u_s = fsb("u_s" + tag, [128, 2048], BF16)
        gu = [fsb(f"gu{i}" + tag, [128, 4096], BF16) for i in range(2)]
        cbc = fsb("cbc" + tag, [128, 1024], BF16)
        rt = fsb("rt" + tag, [128, 1024])
        r_mu = fsb("r_mu" + tag, [1, 512])
        r_var = fsb("r_var" + tag, [1, 512])
        r_rstd = fsb("r_rstd" + tag, [1, 512])
        mu_bc = fsb("mu_bc" + tag, [128, 512])
        rstd_bc = fsb("rstd_bc" + tag, [128, 512])
        rows = (r_mu, r_var, r_rstd)
        bcasts = (mu_bc, rstd_bc)

        trk = {
            "xin": None, "u32": None, "lnx": [None, None],
            "silu_g": None, "u_s": None, "gu0": None, "gu1": None, "cbc": None,
            "combT": None, "rt_sig": None, "rt_tr": None,
            "slot_free": [None, None, None],
            "slot_tok": [None, None, None],
            "wd_free": [None] * 4, "wd_tok": [None] * 4,
        }

        def load_expert(L):
            s = L % 3
            j = L % NJ
            sd = L % 4
            if fill and L < NJ:
                b.wait("P", trk["slot_free"][s], trk["wd_free"][sd], dFill.tok())
                if j >= 2:
                    e = j - 2
                    srcs = [d_wg[e], d_wu[e], d_wd[e]]
                else:
                    srcs = [d_shg[:, j * 512:(j + 1) * 512],
                            d_shu[:, j * 512:(j + 1) * 512],
                            d_shd[j * 512:(j + 1) * 512, :]]
                for wi, s_ap in enumerate(srcs):
                    if wi < 2:
                        inap = s_ap.rearrange("(k p) i -> p k i", p=128)
                        outap = wgu[s][:, wi * 4096:(wi + 1) * 4096]\
                            .rearrange("p (k i) -> p k i", i=512)
                        dW[s].inc(P_.dma_start(out=outap, in_=inap), 16)
                    else:
                        inap = s_ap.rearrange("(k p) h -> p k h", p=128)
                        outap = wds[sd][:].rearrange("p (k h) -> p k h", h=1024)
                        dWd[sd].inc(P_.dma_start(out=outap, in_=inap), 16)
                b.wait("Y", dW[s].tok(), dWd[sd].tok())
                dFill.inc(Y_.dma_start(out=d_wcache[j][:, 0:8192], in_=wgu[s][:]), 16)
                dFill.inc(Y_.dma_start(out=d_wcache[j][:, 8192:12288], in_=wds[sd][:]), 16)
            else:
                b.wait("Y", trk["slot_free"][s], trk["wd_free"][sd])
                if fill:
                    b.wait("Y", dFill.tok())
                dW[s].inc(Y_.dma_start(out=wgu[s][:], in_=d_wcache[j][:, 0:8192]), 16)
                dWd[sd].inc(Y_.dma_start(out=wds[sd][:], in_=d_wcache[j][:, 8192:12288]), 16)
            trk["slot_tok"][s] = dW[s].tok()
            trk["wd_tok"][sd] = dWd[sd].tok()

        loaded = [0]

        def ensure(L):
            while loaded[0] <= L:
                load_expert(loaded[0])
                loaded[0] += 1

        ensure(2)

        for q in range(NQ):
            t_gates = [None, None]
            for tq in range(2):
                col0 = q * QT + tq * 512
                b.wait("Y", trk["xin"], dSt.tok())
                tx = dX.inc(Y_.dma_start(
                    out=xin[:].rearrange("p (m t) -> p m t", t=512),
                    in_=src[:, col0:col0 + 512].rearrange("(m p) t -> p m t", p=128)), 16)
                b.wait("T", tx)
                b.wait("S", tx)
                b.wait("V", trk["u32"])
                t_norm = layer_norm(xin, 8, u32, rows, bcasts, sq,
                                    bank=pDown[:, 0:512], bank_key="down_free")
                trk["xin"] = t_norm
                b.wait("S", t_norm, trk["lnx"][tq])
                lnap = ln_x[:].rearrange("p (m t) -> p m t", t=QT)[:, :, tq * 512:tq * 512 + 512]
                t_cast = cS.inc(S_.copy(lnap, u32[:].rearrange("p (m t) -> p m t", t=512)))
                trk["lnx"][tq] = t_cast
                b.wait("T", t_norm, state["misc_free"])
                for tt in range(4):
                    for m in range(8):
                        inst = T_.matmul(
                            pMisc[:, tq * 128 + tt * 32:tq * 128 + tt * 32 + 32],
                            u32[:, m * 512 + tt * 128: m * 512 + tt * 128 + 128],
                            p_gateW[:, m * 32:(m + 1) * 32],
                            start=(m == 0), stop=(m == 7))
                t_gates[tq] = cT.inc(inst)
                trk["xin"] = [t_cast, t_gates[tq], t_norm]
            state["misc_free"] = None

            def emit_routing(tq):
                t_gate = t_gates[tq]
                b.wait("S", t_gate, trk["rt_sig"], trk["rt_tr"])
                for tt in range(4):
                    S_.activation(rt[:, tt * 256:tt * 256 + 32],
                                  pMisc[:, tq * 128 + tt * 32:tq * 128 + tt * 32 + 32],
                                  AF.Sigmoid)
                t_sig = cS.inc(S_.drain())
                if tq == 1:
                    state["misc_free"] = t_sig

                def ro(tt, a, bb=None):
                    return rt[:, tt * 256 + a:tt * 256 + (bb if bb is not None else a + 1)]

                def ros(tt, a, bb, step):
                    return rt[:, tt * 256 + a:tt * 256 + bb:step]

                b.wait("V", t_sig, state["gb_tok"], trk["rt_tr"])
                for tt in range(4):
                    V_.tensor_add(ro(tt, 32, 64), ro(tt, 0, 32), p_gbbc[:])
                V_.drain()
                for tt in range(4):
                    for pi, (pa, pb) in enumerate(
                            [(0, 1), (0, 2), (0, 3), (1, 2), (1, 3), (2, 3)]):
                        V_.tensor_add(ro(tt, 64 + pi * 8, 72 + pi * 8),
                                      ros(tt, 32 + pa, 64, 4), ros(tt, 32 + pb, 64, 4))
                V_.drain()
                for tt in range(4):
                    V_.tensor_max(ro(tt, 112, 120), ro(tt, 64, 72), ro(tt, 72, 80))
                    V_.tensor_max(ro(tt, 120, 128), ro(tt, 80, 88), ro(tt, 88, 96))
                    V_.tensor_max(ro(tt, 240, 248), ro(tt, 96, 104), ro(tt, 104, 112))
                V_.drain()
                for tt in range(4):
                    V_.tensor_max(ro(tt, 112, 120), ro(tt, 112, 120), ro(tt, 120, 128))
                V_.drain()
                for tt in range(4):
                    V_.tensor_max(ro(tt, 112, 120), ro(tt, 112, 120), ro(tt, 240, 248))
                V_.drain()
                for tt in range(4):
                    V_.max(ro(tt, 120, 128), ro(tt, 112, 120))
                V_.drain()
                for tt in range(4):
                    V_.tensor_scalar(ro(tt, 128, 136), ro(tt, 112, 120),
                                     ro(tt, 123, 124), None, ALU.is_ge)
                V_.drain()
                for tt in range(4):
                    for a in range(4):
                        V_.tensor_copy(ros(tt, 136 + a, 168, 4), ro(tt, 128, 136))
                V_.drain()
                for tt in range(4):
                    V_.tensor_mul(ro(tt, 168, 200), ro(tt, 32, 64), ro(tt, 136, 168))
                V_.drain()
                for tt in range(4):
                    V_.max(ro(tt, 120, 128), ro(tt, 168, 200))
                V_.drain()
                for tt in range(4):
                    V_.tensor_scalar(ro(tt, 200, 232), ro(tt, 168, 200),
                                     ro(tt, 127, 128), None, ALU.is_ge)
                V_.drain()
                for tt in range(4):
                    V_.tensor_mul(ro(tt, 200, 232), ro(tt, 200, 232), ro(tt, 0, 32))
                V_.drain()
                for tt in range(4):
                    V_.reduce_sum(ro(tt, 254), ro(tt, 200, 232), AX.X)
                V_.drain()
                for tt in range(4):
                    V_.tensor_scalar_add(ro(tt, 254), ro(tt, 254), 1e-20)
                V_.drain()
                for tt in range(4):
                    V_.reciprocal(ro(tt, 255), ro(tt, 254))
                V_.drain()
                for tt in range(4):
                    V_.tensor_scalar(ro(tt, 160, 192), ro(tt, 200, 232),
                                     ro(tt, 255), 2.5, ALU.mult, ALU.mult)
                t_comb = cV.inc(V_.drain())
                trk["rt_sig"] = t_comb
                for tt in range(4):
                    b.wait("T", t_comb, state["row_free"])
                    t_tr = cT.inc(T_.transpose(pRow[0:32, 0:128],
                                               rt[:, tt * 256 + 160:tt * 256 + 192],
                                               c_ident[:]))
                    trk["rt_tr"] = t_tr
                    b.wait("S", t_tr, trk["combT"])
                    t_ct = cS.inc(S_.copy(
                        combT[:, tq * 512 + tt * 128:tq * 512 + tt * 128 + 128],
                        pRow[0:32, 0:128]))
                    state["row_free"] = t_ct


            if q == 0:
                tap("lnx", ln_x[:], (128, 8 * QT), BF16)
                tap("combT", combT[:], (32, QT), BF16)
            gu_free = state["gu_free"]
            pend = None

            def emit_down(p):
                P_i, tq_i, sd0_i, sd1_i, toks = p
                for t2 in range(2):
                    b.wait("T", state["down_free"], trk["wd_tok"][sd0_i],
                           trk["wd_tok"][sd1_i], *toks)
                    for hm in range(8):
                        for ji, sd in [(0, sd0_i), (1, sd1_i)]:
                            for kc in range(4):
                                inst = T_.matmul(
                                    pDown[:, hm * 256:(hm + 1) * 256],
                                    wds[sd][:, kc * 1024 + hm * 128:kc * 1024 + hm * 128 + 128],
                                    gu[tq_i][:, ji * 2048 + kc * 512 + t2 * 256:ji * 2048 + kc * 512 + t2 * 256 + 256],
                                    start=(ji == 0 and kc == 0),
                                    stop=(ji == 1 and kc == 3))
                    t_dn = cT.inc(inst)
                    trk["gu" + str(tq_i)] = t_dn
                    off = tq_i * 512 + t2 * 256
                    rap = racc[:].rearrange("p (h t) -> p h t", t=QT)[:, :, off:off + 256]
                    pap = pDown[:].rearrange("p (h t) -> p h t", t=256)
                    b.wait("V", t_dn)
                    if P_i == 0:
                        t_ra = cV.inc(V_.tensor_copy(rap, pap))
                    else:
                        t_ra = cV.inc(V_.tensor_tensor(rap, rap, pap, ALU.add))
                    state["down_free"] = t_ra
                trk["wd_free"][sd0_i] = cT.tok()
                trk["wd_free"][sd1_i] = cT.tok()

            for P in range(NPAIR):
                L0 = q * NJ + 2 * P
                ensure(L0 + 1)
                s0, s1 = L0 % 3, (L0 + 1) % 3
                sd0, sd1 = L0 % 4, (L0 + 1) % 4
                for tq in range(2):
                    gu_toks = []
                    for ji, (LL, s) in enumerate([(L0, s0), (L0 + 1, s1)]):
                        j = LL % NJ
                        real = j >= 2
                        sg_toks = []
                        for mi in range(4):
                            bank = pGU[mi % 2]
                            b.wait("T", gu_free[mi % 2], trk["slot_tok"][s],
                                   trk["lnx"][tq])
                            for k in range(8):
                                inst = T_.matmul(
                                    bank[:],
                                    wgu[s][:, k * 512 + mi * 128:k * 512 + mi * 128 + 128],
                                    ln_x[:, k * QT + tq * 512:k * QT + tq * 512 + 512],
                                    start=(k == 0), stop=(k == 7))
                            t_g = cT.inc(inst)
                            b.wait("S", t_g, trk["silu_g"])
                            t_sg = cS.inc(S_.activation(
                                silu_g[:, mi * 512:(mi + 1) * 512], bank[:], AF.Silu))
                            gu_free[mi % 2] = t_sg
                            sg_toks.append(t_sg)
                        if real:
                            e = j - 2
                            b.wait("T", state["misc_free"], trk["combT"])
                            t_mmc = cT.inc(T_.matmul(
                                pMisc[:], c_sel[0:32, e * 128:(e + 1) * 128],
                                combT[:, tq * 512:tq * 512 + 512],
                                start=True, stop=True))
                            b.wait("S", t_mmc, trk["cbc"])
                            t_cbc = cS.inc(S_.copy(cbc[:, ji * 512:(ji + 1) * 512],
                                                   pMisc[:]))
                            state["misc_free"] = t_cbc
                        else:
                            t_cbc = None
                        us_toks = []
                        for mi in range(4):
                            bank = pGU[mi % 2]
                            b.wait("T", gu_free[mi % 2])
                            for k in range(8):
                                inst = T_.matmul(
                                    bank[:],
                                    wgu[s][:, 4096 + k * 512 + mi * 128:4096 + k * 512 + mi * 128 + 128],
                                    ln_x[:, k * QT + tq * 512:k * QT + tq * 512 + 512],
                                    start=(k == 0), stop=(k == 7))
                            t_u = cT.inc(inst)
                            b.wait("V", t_u, trk["u_s"], t_cbc)
                            if real:
                                t_us = cV.inc(V_.tensor_mul(
                                    u_s[:, mi * 512:(mi + 1) * 512], bank[:],
                                    cbc[:, ji * 512:(ji + 1) * 512]))
                            else:
                                t_us = cV.inc(V_.tensor_copy(
                                    u_s[:, mi * 512:(mi + 1) * 512], bank[:]))
                            gu_free[mi % 2] = t_us
                            us_toks.append(t_us)
                        trk["cbc"] = us_toks[-1]
                        b.wait("V", trk["gu" + str(tq)], *sg_toks)
                        t_gu = cV.inc(V_.tensor_mul(
                            gu[tq][:, ji * 2048:(ji + 1) * 2048], silu_g[:], u_s[:]))
                        gu_toks.append(t_gu)
                        trk["silu_g"] = t_gu
                        trk["u_s"] = t_gu
                    if pend is not None:
                        emit_down(pend)
                    pend = (P, tq, sd0, sd1, gu_toks)
                if P == 0:
                    emit_routing(0)
                    emit_routing(1)
                trk["slot_free"][s0] = cT.tok()
                trk["slot_free"][s1] = cT.tok()
                ensure(L0 + 3)
            if pend is not None:
                emit_down(pend)
                pend = None
            state["gu_free"] = gu_free
            trk["combT"] = cT.tok()
            trk["lnx"] = [cT.tok(), cT.tok()]

            for tq in range(2):
                col0 = q * QT + tq * 512
                b.wait("Y", trk["xin"])
                tx = dX.inc(Y_.dma_start(
                    out=xin[:].rearrange("p (m t) -> p m t", t=512),
                    in_=src[:, col0:col0 + 512].rearrange("(m p) t -> p m t", p=128)), 16)
                rap = racc[:].rearrange("p (h t) -> p h t", t=QT)[:, :, tq * 512:tq * 512 + 512]
                b.wait("V", tx, state["down_free"], dSt.tok())
                t_xo = cV.inc(V_.scalar_tensor_tensor(
                    xin[:].rearrange("p (m t) -> p m t", t=512), rap, 0.5,
                    xin[:].rearrange("p (m t) -> p m t", t=512), ALU.mult, ALU.add))
                trk["xin"] = t_xo
                b.wait("Y", t_xo)
                dSt.inc(Y_.dma_start(
                    out=dst[:, col0:col0 + 512].rearrange("(m p) t -> p m t", p=128),
                    in_=xin[:].rearrange("p (m t) -> p m t", t=512)), 16)
                state["down_free"] = t_xo
        f.close()

    def mamba(mdst):
        f = ExitStack()
        fsb = lambda n, sh, dt=F32: f.enter_context(nc.sbuf_tensor(n, sh, dt))
        mw_in = [fsb(f"mw_in{i}", [128, 1024], BF16) for i in range(2)]
        mw_out = [fsb(f"mw_out{i}", [128, 2048], BF16) for i in range(2)]
        m_ubf = fsb("m_ubf", [128, 4096], BF16)
        xin = fsb("m_xin", [128, 4096])
        u32 = xin
        sq = fsb("m_sq", [128, 4096], BF16)
        m_xm = fsb("m_xm", [128, 515])
        m_cacc2 = [fsb(f"m_cacc{i}", [128, 512]) for i in range(2)]
        m_zs = fsb("m_zs", [128, 8192], BF16)
        m_xc = fsb("m_xc", [128, 8192], BF16)
        m_ssm = fsb("m_ssm", [33, 512])
        m_ssm_bf = fsb("m_ssm_bf", [33, 512], BF16)
        m_dt = fsb("m_dt", [1, 512])
        m_Bbc = [fsb(f"m_Bbc{i}", [128, 512], BF16) for i in range(2)]
        m_Cbc = [fsb(f"m_Cbc{i}", [128, 512], BF16) for i in range(2)]
        m_dtB = [fsb(f"m_dtB{i}", [128, 512], BF16) for i in range(2)]
        m_dtbc = fsb("m_dtbc", [128, 512])
        m_dA4 = [fsb(f"m_dA4{i}", [128, 2048], BF16) for i in range(4)]
        m_dBx4 = [fsb("m_dBx40", [128, 2048], BF16)] * 2
        m_h4 = [fsb(f"m_h4{i}", [128, 2048], BF16) for i in range(3)]
        m_hC4 = [fsb(f"m_hC4{i}", [128, 2048], BF16) for i in range(4)]

        def rep4(t512):
            ap = t512[:]
            return type(ap)(tensor=ap.tensor, offset=ap.offset,
                            ap=[list(ap.ap[0]), [0, 4], [1, 512]])

        def g3(ap2d):
            return ap2d.rearrange("p (g t) -> p g t", t=512)
        m_yacc = fsb("m_yacc", [128, 8192], BF16)
        m_yfin = m_yacc
        m_r32 = fsb("m_r32", [128, 4096])
        r_mu = fsb("mr_mu", [1, 512])
        r_var = fsb("mr_var", [1, 512])
        r_rstd = fsb("mr_rstd", [1, 512])
        m_rowa = r_rstd
        m_rowb = r_var
        mu_bc = fsb("mmu_bc", [128, 512])
        rstd_bc = fsb("mrstd_bc", [128, 512])
        rows = (r_mu, r_var, r_rstd)
        bcasts = (mu_bc, rstd_bc)
        m_carry = fsb("m_carry", [128, 256])
        m_tail = fsb("m_tail", [128, 48])
        dZ = b.ctr("dZ" + str(len(str(taps))))
        dZ.inc(Y_.dma_start(out=m_carry[:], in_=d_zeros[:, 0:256]), 16)
        dZ.inc(Y_.dma_start(out=m_tail[:], in_=d_zeros[:, 0:48]), 16)

        fill_free = [None, None, None, None]
        for m in range(32):
            s = m % 2
            b.wait("P", fill_free[s])
            dMw[s].inc(P_.dma_start(
                out=mw_in[s][:].rearrange("p (k c) -> p k c", c=128),
                in_=d_inW[:, m * 128:(m + 1) * 128].rearrange("(k p) c -> p k c", p=128)), 16)
            b.wait("Y", dMw[s].tok())
            fill_free[s] = dMC.inc(Y_.dma_start(out=d_mcin[m], in_=mw_in[s][:]), 16)
        for dm in range(8):
            s = dm % 2
            b.wait("P", fill_free[2 + s])
            dMw[2 + s].inc(P_.dma_start(
                out=mw_out[s][:].rearrange("p (k c) -> p k c", c=128),
                in_=d_outW[:, dm * 128:(dm + 1) * 128].rearrange("(k p) c -> p k c", p=128)), 16)
            b.wait("Y", dMw[2 + s].tok())
            fill_free[2 + s] = dMC.inc(Y_.dma_start(out=d_mcout[dm], in_=mw_out[s][:]), 16)
        b.wait("Y", dMC.tok())

        trk = {"xin": None, "u32": None, "ubf": None, "zs": None,
               "xc": None, "xm": None, "cacc": [None, None], "tail": dZ.tok(),
               "slot": [None, None], "oslot": [None, None],
               "yacc": [None] * 16, "yfin": None, "r32": None, "ssm": None,
               "dA_free": [None] * 4, "dBx_free": [None, None],
               "h_free": [None] * 3, "hC_free": [None] * 4,
               "Bbc": [None, None], "Cbc": [None, None],
               "dtB": [None, None], "dtbc": None}
        zs_tok = [None] * 16
        b.wait("V", dZ.tok())

        for c in range(NCH):
            col0 = c * 512
            b.wait("Y", trk["xin"], dSt.tok())
            tx = dX.inc(Y_.dma_start(
                out=xin[:].rearrange("p (m t) -> p m t", t=512),
                in_=d_resid[:, col0:col0 + 512].rearrange("(m p) t -> p m t", p=128)), 16)
            b.wait("T", tx)
            b.wait("S", tx)
            b.wait("V", trk["u32"])
            t_norm = layer_norm(xin, 8, u32, rows, bcasts, sq)
            b.wait("S", t_norm, trk["ubf"])
            t_ubf = cS.inc(S_.copy(m_ubf[:], u32[:]))
            trk["u32"] = [t_ubf]
            b.wait("Y", t_ubf)
            tx = dX.inc(Y_.dma_start(
                out=xin[:].rearrange("p (m t) -> p m t", t=512),
                in_=d_resid[:, col0:col0 + 512].rearrange("(m p) t -> p m t", p=128)), 16)

            morder = [mm for pr in zip(range(16), range(16, 32)) for mm in pr]
            for mi_, m in enumerate(morder):
                s = mi_ % 2
                bank = pGU[s]
                b.wait("Y", trk["slot"][s])
                dMw[s].inc(Y_.dma_start(out=mw_in[s][:], in_=d_mcin[m]), 16)
                b.wait("T", dMw[s].tok(), t_ubf, state["gu_free"][s])
                for k in range(8):
                    inst = T_.matmul(bank[:], mw_in[s][:, k * 128:(k + 1) * 128],
                                     m_ubf[:, k * 512:(k + 1) * 512],
                                     start=(k == 0), stop=(k == 7))
                t_mmi = cT.inc(inst)
                trk["slot"][s] = t_mmi
                if m < 16:
                    k_ = m
                    m_cacc = m_cacc2[k_ % 2]
                    b.wait("V", t_mmi, trk["xm"], trk["tail"],
                           trk["cacc"][k_ % 2])
                    V_.tensor_copy(m_xm[:, 0:3], m_tail[:, k_ * 3:k_ * 3 + 3])
                    t_ev = cV.inc(V_.tensor_scalar_add(m_xm[:, 3:515], bank[:],
                                                       p_inB[:, m:m + 1]))
                    state["gu_free"][s] = t_ev
                    V_.tensor_scalar_mul(m_cacc[:], m_xm[:, 0:512],
                                         p_convw[:, k_ * 4:k_ * 4 + 1])
                    for jj in range(1, 4):
                        V_.scalar_tensor_tensor(
                            m_cacc[:], m_xm[:, jj:jj + 512],
                            p_convw[:, k_ * 4 + jj:k_ * 4 + jj + 1],
                            m_cacc[:], ALU.mult, ALU.add)
                    t_cv = cV.inc(V_.tensor_copy(m_tail[:, k_ * 3:k_ * 3 + 3],
                                                 m_xm[:, 512:515]))
                    trk["tail"] = t_cv
                    trk["xm"] = t_cv
                    b.wait("S", t_cv, trk["xc"])
                    t_xc = cS.inc(S_.activation(m_xc[:, k_ * 512:(k_ + 1) * 512],
                                                m_cacc[:], AF.Silu,
                                                bias=p_convb[:, k_:k_ + 1]))
                    trk["cacc"][k_ % 2] = t_xc
                else:
                    k_ = m - 16
                    b.wait("S", t_mmi, trk["zs"])
                    t_z = cS.inc(S_.activation(m_zs[:, k_ * 512:(k_ + 1) * 512],
                                               bank[:], AF.Silu,
                                               bias=p_inB[:, m:m + 1]))
                    zs_tok[k_] = t_z
                    state["gu_free"][s] = t_z
            t_xc_all = trk["cacc"][1]

            b.wait("T", t_xc_all, state["row_free"], dInit.tok(), dInitP.tok())
            for k in range(16):
                inst = T_.matmul(pRow[:], p_xpW[:, k * 33:(k + 1) * 33],
                                 m_xc[:, k * 512:(k + 1) * 512],
                                 start=(k == 0), stop=(k == 15))
            t_ssm = cT.inc(inst)
            b.wait("V", t_ssm, trk["ssm"])
            t_ssme = cV.inc(V_.tensor_scalar_add(m_ssm[:], pRow[:], p_xpB[:]))
            state["row_free"] = t_ssme
            b.wait("S", t_ssme, trk["ssm"])
            t_ssmbf = cS.inc(S_.copy(m_ssm_bf[:], m_ssm[:]))
            b.wait("S", t_ssme)
            S_.activation(m_rowa[:], m_ssm[32:33, :], AF.Abs)
            S_.activation(m_rowa[:], m_rowa[:], AF.Exp, scale=-1.0)
            t_ln = cS.inc(S_.activation(m_rowa[:], m_rowa[:], AF.Ln, bias=1.0))
            b.wait("V", t_ln)
            V_.tensor_scalar_max(m_rowb[:], m_ssm[32:33, :], 0.0)
            t_dt = cV.inc(V_.tensor_add(m_dt[:], m_rowa[:], m_rowb[:]))
            b.wait("T", t_dt, state["misc_free"], trk["dtbc"])
            t_bcm = cT.inc(T_.matmul(pMisc[:], c_ones1[:], m_dt[:],
                                     start=True, stop=True))
            b.wait("S", t_bcm)
            t_dtbc = cS.inc(S_.copy(m_dtbc[:], pMisc[:]))
            state["misc_free"] = t_dtbc

            for k in range(16):
                b.wait("V", t_xc_all, trk["yacc"][k], state["A_tok"] if c == 0 else None)
                trk["yacc"][k] = cV.inc(V_.tensor_scalar_mul(
                    m_yacc[:, k * 512:(k + 1) * 512],
                    m_xc[:, k * 512:(k + 1) * 512], p_D[:, k:k + 1]))

            def prep_n(n):
                nb = n % 2
                b.wait("T", t_ssmbf, state["down_free"], trk["Bbc"][nb],
                       trk["Cbc"][nb])
                T_.matmul(pDown[:, 0:512], c_sel[:, n * 128:(n + 1) * 128],
                          m_ssm_bf[:], start=True, stop=True)
                t_bc1 = cT.inc(T_.matmul(pDown[:, 512:1024],
                                         c_sel[:, (16 + n) * 128:(17 + n) * 128],
                                         m_ssm_bf[:], start=True, stop=True))
                b.wait("S", t_bc1)
                t_B = cS.inc(S_.copy(m_Bbc[nb][:], pDown[:, 0:512]))
                t_C = cS.inc(S_.copy(m_Cbc[nb][:], pDown[:, 512:1024]))
                state["down_free"] = t_C
                return t_B, t_C

            def dtb_n(n, t_B):
                nb = n % 2
                b.wait("V", t_B, t_dtbc, trk["dtB"][nb])
                t_dtB = cV.inc(V_.tensor_mul(m_dtB[nb][:], m_Bbc[nb][:],
                                             m_dtbc[:]))
                trk["Bbc"][nb] = t_dtB
                return t_dtB

            tB0, tC0 = prep_n(0)
            t_dtB_cur = dtb_n(0, tB0)
            t_C_cur = tC0
            ti = 0
            for n in range(NS):
                nb = n % 2
                if n + 1 < NS:
                    nxt = prep_n(n + 1)
                for g4 in range(4):
                    db = ti % 4
                    xb = ti % 2
                    hb = ti % 3
                    cb = ti % 4
                    b.wait("S", t_dtbc, trk["dA_free"][db])
                    for kk in range(4):
                        idx = (g4 * 4 + kk) * 16 + n
                        inst = S_.activation(
                            m_dA4[db][:, kk * 512:(kk + 1) * 512],
                            m_dtbc[:], AF.Exp, scale=p_A[:, idx:idx + 1])
                    t_dA = cS.inc(inst)
                    b.wait("V", t_dtB_cur, t_xc_all, trk["dBx_free"][xb])
                    t_dBx = cV.inc(V_.tensor_tensor(
                        g3(m_dBx4[xb][:]),
                        g3(m_xc[:, g4 * 2048:(g4 + 1) * 2048]),
                        rep4(m_dtB[nb]), ALU.mult))
                    for kk in range(4):
                        k = g4 * 4 + kk
                        idx = k * 16 + n
                        b.wait("V", t_dA, trk["h_free"][hb])
                        t_h = cV.inc(V_.tensor_tensor_scan(
                            m_h4[hb][:, kk * 512:(kk + 1) * 512],
                            m_dA4[db][:, kk * 512:(kk + 1) * 512],
                            m_dBx4[xb][:, kk * 512:(kk + 1) * 512],
                            m_carry[:, idx:idx + 1], ALU.mult, ALU.add))
                        trk["dA_free"][db] = t_h
                    trk["dBx_free"][xb] = cV.tok()
                    for kk in range(4):
                        idx = (g4 * 4 + kk) * 16 + n
                        V_.tensor_copy(m_carry[:, idx:idx + 1],
                                       m_h4[hb][:, kk * 512 + 511:kk * 512 + 512])
                    t_hdone = cV.tok()
                    b.wait("V", t_C_cur, trk["hC_free"][cb])
                    t_hC = cV.inc(V_.tensor_tensor(
                        g3(m_hC4[cb][:]), g3(m_h4[hb][:]),
                        rep4(m_Cbc[nb]), ALU.mult))
                    trk["h_free"][hb] = t_hC
                    trk["Cbc"][nb] = t_hC
                    b.wait("V", t_hC, trk["yacc"][g4 * 4])
                    t_y = cV.inc(V_.tensor_tensor(
                        m_yacc[:, g4 * 2048:(g4 + 1) * 2048],
                        m_yacc[:, g4 * 2048:(g4 + 1) * 2048], m_hC4[cb][:], ALU.add))
                    for kk in range(4):
                        trk["yacc"][g4 * 4 + kk] = t_y
                    trk["hC_free"][cb] = t_y
                    ti += 1
                if n + 1 < NS:
                    t_dtB_cur = dtb_n(n + 1, nxt[0])
                    t_C_cur = nxt[1]
            trk["xc"] = [cV.tok(), cP.tok(), t_ssm]
            trk["dtbc"] = [cV.tok(), cS.tok()]
            trk["ssm"] = [cT.tok(), t_ln]

            for k in range(16):
                b.wait("V", trk["yacc"][k], zs_tok[k], trk["yfin"])
                cV.inc(V_.tensor_mul(m_yfin[:, k * 512:(k + 1) * 512],
                                     m_yacc[:, k * 512:(k + 1) * 512],
                                     m_zs[:, k * 512:(k + 1) * 512]))
            t_yf = cV.tok()
            trk["zs"] = t_yf
            for kk in range(16):
                trk["yacc"][kk] = t_yf
            for dm in range(8):
                s = dm % 2
                bank = pGU[s]
                b.wait("Y", trk["oslot"][s])
                dMw[2 + s].inc(Y_.dma_start(out=mw_out[s][:], in_=d_mcout[dm]), 16)
                b.wait("T", dMw[2 + s].tok(), t_yf, state["gu_free"][s])
                for k in range(16):
                    inst = T_.matmul(bank[:], mw_out[s][:, k * 128:(k + 1) * 128],
                                     m_yfin[:, k * 512:(k + 1) * 512],
                                     start=(k == 0), stop=(k == 15))
                t_op = cT.inc(inst)
                trk["oslot"][s] = t_op
                b.wait("V", t_op, trk["r32"], t_ubf)
                t_r = cV.inc(V_.scalar_tensor_tensor(
                    m_r32[:, dm * 512:(dm + 1) * 512], bank[:],
                    p_outB[:, dm:dm + 1], m_ubf[:, dm * 512:(dm + 1) * 512],
                    ALU.add, ALU.add))
                state["gu_free"][s] = t_r
            t_rall = cV.tok()
            trk["ubf"] = t_rall
            trk["yfin"] = cT.tok()
            b.wait("T", t_rall)
            b.wait("S", t_rall)
            t_n2 = layer_norm(m_r32, 8, m_r32, rows, bcasts, sq)
            trk["r32"] = [t_n2, cT.tok()]
            b.wait("V", t_n2, tx)
            for dm in range(8):
                sl = slice(dm * 512, (dm + 1) * 512)
                V_.tensor_scalar(m_r32[:, sl], m_r32[:, sl], p_gamma[:, dm:dm + 1],
                                 p_beta[:, dm:dm + 1], ALU.mult, ALU.add)
                inst = V_.tensor_add(xin[:, sl], m_r32[:, sl], xin[:, sl])
            t_x2 = cV.inc(inst)
            trk["u32"] = [t_x2]
            trk["xin"] = t_x2
            for kk in range(16):
                trk["yacc"][kk] = [t_x2, cT.tok(), cS.tok()]
            b.wait("Y", t_x2)
            dSt.inc(Y_.dma_start(
                out=mdst[:, col0:col0 + 512].rearrange("(m p) t -> p m t", p=128),
                in_=xin[:].rearrange("p (m t) -> p m t", t=512)), 16)
        f.close()

    with es, nc.Block() as blk:
        @blk.vector
        def _(v):
            last = stages[-1]
            ffn_call(d_x0, d_out if last == "ffn1" else d_resid, fill=True)
            if "mamba" in stages:
                b.join(extra=[dX, dSt, dFill] + dW + dWd)
                mamba(d_out if last == "mamba" else d_resid)
            if "ffn2" in stages:
                b.join(extra=[dX, dSt, dMC] + dMw)
                ffn_call(d_resid, d_out, fill=False)
            Y_.wait_ge(dSt.sem, dSt.n)
    return nc


def _host_prep(inp):
    asc = np.ascontiguousarray
    f = lambda a: np.asarray(a, np.float32)
    g = {}
    g["inWT"] = asc(f(inp["L0_attn_in_proj_w"]).T)
    g["outWT"] = asc(f(inp["L0_attn_out_proj_w"]).T)
    g["xpWT"] = asc(f(inp["L0_attn_x_proj_w"]).T)
    g["gateWt"] = asc(f(inp["L0_ffn_gate_w"]).T.reshape(8, 128, 32)
                      .transpose(1, 0, 2).reshape(128, 256))
    g["shgT"] = asc(f(inp["L0_ffn_sh_gate"]).T)
    g["shuT"] = asc(f(inp["L0_ffn_sh_up"]).T)
    g["shdT"] = asc(f(inp["L0_ffn_sh_down"]).T)
    g["wgT"] = asc(f(inp["L0_ffn_w_gate"]).transpose(0, 2, 1))
    g["wuT"] = asc(f(inp["L0_ffn_w_up"]).transpose(0, 2, 1))
    g["wdT"] = asc(f(inp["L0_ffn_w_down"]).transpose(0, 2, 1))
    g["At"] = asc(f(inp["L0_attn_A_log"]).reshape(16, 128, 16)
                  .transpose(1, 0, 2).reshape(128, 256))
    g["convwT"] = asc(f(inp["L0_attn_conv_w"]).reshape(2048, 4)
                      .reshape(16, 128, 4).transpose(1, 0, 2).reshape(128, 64))
    g["convbT"] = asc(f(inp["L0_attn_conv_b"]).reshape(16, 128).T)
    g["Dt"] = asc(f(inp["L0_attn_D"]).reshape(16, 128).T)
    g["inBt"] = asc(f(inp["L0_attn_in_proj_b"]).reshape(32, 128).T)
    g["outBt"] = asc(f(inp["L0_attn_out_proj_b"]).reshape(8, 128).T)
    g["xpBt"] = asc(f(inp["L0_attn_x_proj_b"]).reshape(33, 1))
    g["gammat"] = asc(f(inp["L0_attn_ln_gamma"]).reshape(8, 128).T)
    g["betat"] = asc(f(inp["L0_attn_ln_beta"]).reshape(8, 128).T)
    g["gateBt"] = asc(f(inp["L0_ffn_gate_bias"]).reshape(1, 32))
    g["ones1"] = np.ones((1, 128), np.float32)
    g["onecol"] = np.ones((128, 1), np.float32)
    g["ident"] = np.eye(128, dtype=np.float32)
    sel = np.zeros((33, 4096), np.float32)
    for r in range(32):
        sel[r, r * 128:(r + 1) * 128] = 1.0
    g["sel33"] = sel
    g["zeros"] = np.zeros((128, 256), np.float32)
    return g


def kernel(**inputs):
    shared = _host_prep(inputs)
    x = np.asarray(inputs["L0_attn_x"], np.float32)
    in_maps = []
    for bb in range(8):
        m = dict(shared)
        m["x0T"] = np.ascontiguousarray(x[bb].T)
        in_maps.append(m)
    nc = build(bass.Bass("TRN2", target_bir_lowering=False, debug=False))
    res = run_bass_kernel_spmd(nc, in_maps, core_ids=list(range(8)))
    out = np.stack([np.asarray(r["outT"], np.float32).T for r in res.results])
    return np.ascontiguousarray(out)
